# version 04 (final)
# baseline (speedup 1.0000x reference)
import numpy as np
import ml_dtypes
import concourse.bass as bass
import concourse.mybir as mybir
from concourse.bass import IndirectOffsetOnAxis
from concourse.bass_utils import run_bass_kernel_spmd

F32 = mybir.dt.float32
BF16 = mybir.dt.bfloat16
I32 = mybir.dt.int32
U8 = mybir.dt.uint8
AF = mybir.ActivationFunctionType
ALU = mybir.AluOpType
AX = mybir.AxisListType
BF = ml_dtypes.bfloat16

D = 1024
HD = 64
H = 16
KVH = 4
LAT = 256
S = 2048
NS = 4
NTOK = NS * S
FF = 4096
NE = 16
NL = 4
CAP = 1024
NCAP = NE * CAP
BIGSLOT = float(1 << 20)
NCORES = 8

ENGINES = ("sync", "scalar", "vector", "tensor", "gpsimd")


def _rope_tables(base):
    k = np.arange(0, HD, 2, dtype=np.float64) / float(HD)
    inv = (1.0 / (np.float64(base) ** k)).astype(np.float32)
    t = np.arange(S, dtype=np.float32)
    fr = t[:, None] * inv[None, :]
    cos = np.cos(fr).astype(np.float32)
    sin = np.sin(fr).astype(np.float32)
    cr = np.repeat(cos, 2, axis=1)
    ss = np.empty((S, HD), np.float32)
    ss[:, 0::2] = -sin
    ss[:, 1::2] = sin
    return np.ascontiguousarray(cr), np.ascontiguousarray(ss)


def _masks():
    p = np.arange(128)[:, None]
    jg = np.arange(512)[None, :]
    mg = np.zeros((128, 2048), np.float32)
    for m in range(4):
        mg[:, m * 512:(m + 1) * 512] = (jg >= p + 128 * m)
    jl = np.arange(256)[None, :]
    ml = np.zeros((128, 1024), np.float32)
    for m in range(4):
        ml[:, m * 256:(m + 1) * 256] = (jl >= p + 128 * m - 256) & (jl < p + 128 * m)
    return mg.astype(BF), ml.astype(BF)


def host_consts(inputs):
    crg, ssg = _rope_tables(1.0e6)
    crl, ssl = _rope_tables(1.0e4)
    mg, ml = _masks()
    c = {
        "crg": crg, "ssg": ssg, "crl": crl, "ssl": ssl,
        "maskg": mg, "maskl": ml,
        "identb": np.eye(128, dtype=np.float32).astype(BF),
        "ut128": np.triu(np.ones((128, 128), np.float32), 1),
        "iota16": np.tile(np.arange(NE, dtype=np.float32), (128, 1)),
        "iota128": np.arange(128, dtype=np.float32).reshape(128, 1),
        "ones64": np.ones((1, 64), np.float32),
        "ones128": np.ones((1, 128), np.float32),
        "n1w": np.ascontiguousarray(np.broadcast_to(
            np.asarray(inputs["L0_attn_norm1_w"], np.float32), (128, D))),
        "n2w": np.ascontiguousarray(np.broadcast_to(
            np.asarray(inputs["L0_attn_norm2_w"], np.float32), (128, D))),
    }
    w1 = np.asarray(inputs["L0_ffn_W1"], np.float32)
    w2 = np.asarray(inputs["L0_ffn_W2"], np.float32)
    c["w1p"] = np.ascontiguousarray(
        w1.reshape(NE, 8, 128, 8, 512).transpose(0, 3, 1, 2, 4))
    c["w2p"] = np.ascontiguousarray(
        w2.reshape(NE, 16, 2, 128, 8, 128).transpose(0, 4, 1, 3, 2, 5)
    ).reshape(NE, 8, 16, 128, 256)
    return c


class Sem:
    __slots__ = ("h", "n", "name")

    def __init__(self, nc, name):
        self.h = nc.alloc_semaphore(name)
        self.n = 0
        self.name = name


class Buf:
    __slots__ = ("w", "r", "name")

    def __init__(self, name=""):
        self.w = {}
        self.r = {}
        self.name = name

    def merge_r(self, tok):
        s, v = tok
        if self.r.get(s, 0) < v:
            self.r[s] = v


class Prog:
    def __init__(self, nc):
        self.nc = nc
        self.ops = []
        self.hw = {e: {} for e in ENGINES}
        self.sems = []
        self.esem = {}
        self.csem = {}

    def sem(self, name):
        s = Sem(self.nc, name)
        self.sems.append(s)
        return s

    def _emit(self, eng, fn, waits, inc, inc_k):
        hw = self.hw[eng]
        w2 = []
        for s, v in waits:
            if v and hw.get(s, 0) < v:
                w2.append((s, v))
                hw[s] = v
        if inc is not None:
            inc.n += inc_k
        val = inc.n if inc is not None else None

        def run(e, _w=tuple(w2), _fn=fn, _i=inc, _k=inc_k):
            for s, v in _w:
                e.wait_ge(s.h, v)
            ins = _fn(e)
            if _i is not None:
                ins.then_inc(_i.h, _k)
        self.ops.append((eng, run))
        return val

    def op(self, eng, fn, reads=(), writes=(), waits=(), dma=False):
        wl = list(waits)
        for b in reads:
            wl.extend(b.w.items())
        for b in writes:
            wl.extend(b.w.items())
            wl.extend(b.r.items())
        sem = self.esem[eng] if dma else self.csem[eng]
        k = 16 if dma else 1
        val = self._emit(eng, fn, wl, sem, k)
        tok = (sem, val)
        for b in reads:
            b.merge_r(tok)
        for b in writes:
            b.w = {sem: val}
            b.r = {}
        return tok

    def wait(self, eng, waits):
        hw = self.hw[eng]
        w2 = [(s, v) for s, v in waits if v and hw.get(s, 0) < v]
        for s, v in w2:
            hw[s] = v
        if w2:
            def run(e, _w=tuple(w2)):
                for s, v in _w:
                    e.wait_ge(s.h, v)
            self.ops.append((eng, run))

    def replay(self, name, eng):
        for en, fn in self.ops:
            if en == name:
                fn(eng)


class Alloc:
    def __init__(self):
        self.off = 0
        self.peak = 0

    def take(self, nbytes, align=64):
        o = (self.off + align - 1) // align * align
        self.off = o + nbytes
        self.peak = max(self.peak, self.off)
        return o


ISZ = {F32: 4, BF16: 2, I32: 4, U8: 1}


class K:
    def __init__(self, dbg=0):
        self.dbg = dbg
        nc = bass.Bass("TRN2", target_bir_lowering=False)
        self.nc = nc
        p = Prog(nc)
        self.p = p
        self.bufs = {}
        self.declare_hbm()
        self.declare_sbuf()
        self.pall = nc.alloc_psum_tensor("pall", [128, 4096], F32)

        self.qS = p.sem("qS")
        self.qA = p.sem("qA")
        self.qG = p.sem("qG")
        self.tP = p.sem("tP")
        self.tV = p.sem("tV")
        self.tA = p.sem("tA")
        self.tG = p.sem("tG")
        p.esem = {"sync": self.qS, "scalar": self.qA, "gpsimd": self.qG}
        p.csem = {"tensor": self.tP, "vector": self.tV, "scalar": self.tA,
                  "gpsimd": self.tG, "sync": self.qS}
        self.r_ncap = nc.gpsimd.alloc_register("bndcap")
        self.r_ntok = nc.gpsimd.alloc_register("bndtok")
        first = min(s.h.num for s in p.sems)
        last = max(s.h.num for s in p.sems)
        nc.gpsimd.dma_reset(range(first, last + 1))
        nc.gpsimd.sem_clear(range(first, last + 1))
        nc.all_engine_barrier()

    def declare_hbm(self):
        nc = self.nc

        def din(name, shape, dt):
            return nc.dram_tensor(name, shape, dt, kind="ExternalInput").ap()

        def dint(name, shape, dt):
            return nc.dram_tensor(name, shape, dt, kind="Internal").ap()

        self.hid = din("hid", [NTOK, D], F32)
        for n, sh in [("gwq", [D, D]), ("gkvd", [D, LAT]), ("gkup", [LAT, LAT]),
                      ("gvup", [LAT, LAT]), ("gwo", [D, D]), ("lwq", [D, D]),
                      ("lwk", [D, LAT]), ("lwv", [D, LAT]), ("lwo", [D, D]),
                      ("n1w", [128, D]), ("n2w", [128, D]),
                      ("routers", [NL, D, NE]),
                      ("w1p", [NE, 8, 8, 128, 512]), ("w2p", [NE, 8, 16, 128, 256]),
                      ("crg", [S, HD]), ("ssg", [S, HD]), ("crl", [S, HD]),
                      ("ssl", [S, HD]), ("ut128", [128, 128]),
                      ("iota16", [128, NE]), ("iota128", [128, 1]),
                      ("ones64", [1, 64]), ("ones128", [1, 128])]:
            setattr(self, "i_" + n, din(n, sh, F32))
        self.i_maskg = din("maskg", [128, 2048], BF16)
        self.i_maskl = din("maskl", [128, 1024], BF16)
        self.i_identb = din("identb", [128, 128], BF16)
        self.out = nc.dram_tensor("out", [NTOK, D], F32, kind="ExternalOutput").ap()

        self.u_hbm = dint("u_hbm", [S, D], F32)
        self.a1_hbm = dint("a1_hbm", [S, D], F32)
        self.z1fm_hbm = dint("z1fm_hbm", [16 * 128, 1024], BF16)
        self.z2fm_hbm = dint("z2fm_hbm", [16 * 128, 1024], BF16)
        self.x1_hbm = dint("x1_hbm", [NTOK, D], F32)
        self.xt_hbm = [dint("xtA", [NTOK, D], BF16), dint("xtB", [NTOK, D], BF16)]
        self.gath = dint("gath", [NCAP, D], BF16)
        self.idxmap = dint("idxmap", [NCAP, 1], I32)
        self.wslot = dint("wslot", [NCAP, 1], F32)
        self.wbh = {}
        for n, sh in [("wq", [D, D]), ("kvd", [D, LAT]), ("kup", [LAT, LAT]),
                      ("vup", [LAT, LAT]), ("wo", [D, D]), ("lq", [D, D]),
                      ("lk", [D, LAT]), ("lv", [D, LAT]), ("lo", [D, D])]:
            self.wbh[n] = dint(n + "_b", sh, BF16)
        self.w1b = dint("w1b", [NE, 8, 8, 128, 512], BF16)
        self.w2b = dint("w2b", [NE, 8, 16, 128, 256], BF16)

        self.dbg_t = {}

        def dou(name, shape, dt):
            t = nc.dram_tensor(name, shape, dt, kind="ExternalOutput").ap()
            self.dbg_t[name] = t
        if self.dbg:
            dou("d_z1", [S, D], BF16)
            dou("d_qfm", [8 * 128, 512], BF16)
            dou("d_kfm", [2 * 128, S], BF16)
            dou("d_vsb", [128, 16 * 4 * 65], BF16)
            dou("d_afm", [8 * 128, S], BF16)
            dou("d_a1", [S, D], F32)
            dou("d_x1", [NTOK, D], F32)
            dou("d_xt0", [NTOK, D], BF16)
        if self.dbg >= 3:
            dou("d_gt", [8 * 128, 1024], BF16)
            dou("d_g2", [8 * 128, 1024], BF16)
            dou("d_g", [8 * 128, 1024], BF16)
            dou("d_h", [32 * 128, 1024], BF16)
            dou("d_o2", [128, 1024], BF16)
            dou("d_xtn", [8 * 128, 1024], BF16)
            dou("d_qpre", [128, 1024], F32)
            dou("d_qrope", [128, 1024], BF16)
            dou("d_lat", [S, LAT], BF16)
            dou("d_kpre", [S, LAT], F32)
            dou("d_z1f", [S, 1024], BF16)
        if self.dbg >= 2:
            dou("d_idx", [NTOK, 1], F32)
            dou("d_w", [NTOK, 1], F32)
            dou("d_rank", [NTOK, 1], F32)
            for l in range(NL):
                dou(f"d_xt{l + 1}", [NTOK, D], BF16)

    def declare_sbuf(self):
        nc = self.nc
        self.c = {}
        for n, sh, dt in [("identb", [128, 128], BF16), ("ut", [128, 128], F32),
                          ("iota16", [128, NE], F32), ("iota128", [128, 1], F32),
                          ("ones64", [1, 64], F32), ("ones128", [1, 128], F32),
                          ("ones_col", [128, 1], F32),
                          ("routb", [128, NL * 8 * NE], BF16),
                          ("Rsb", [1, NE], F32), ("rsum", [1, 512], F32),
                          ("rsum1", [1, 512], F32),
                          ("bigfill", [128, 128], I32),
                          ("slots", [128, 64], I32), ("wsb", [128, 64], F32)]:
            self.c[n] = nc.alloc_sbuf_tensor("c_" + n, sh, dt)
        self.small = []
        for par in range(2):
            d = {}
            for n, sh in [("s1", [128, 1]), ("m", [128, 1]), ("s2", [128, 1]),
                          ("rstd1", [128, 1]), ("idx", [128, 1]), ("rank", [128, 1]),
                          ("oflw", [128, 1]), ("slot", [128, 1]), ("mx", [128, 1]),
                          ("sumsq", [128, NE]), ("t16a", [128, NE]), ("t16b", [128, NE]),
                          ("rstd", [128, NE]), ("eq", [128, NE]), ("oh", [128, NE]),
                          ("pb", [128, NE])]:
                d[n] = nc.alloc_sbuf_tensor(f"s{par}_" + n, sh, F32)
            d["tok"] = nc.alloc_sbuf_tensor(f"s{par}_tok", [128, 1], I32)
            self.small.append(d)

        self.AR = nc.alloc_sbuf_tensor("arena", [128, 200704], U8)
        a = Alloc()
        t = a.take
        self.o_zfm = [t(2048) for _ in range(2)]
        self.o_qbf = [t(2048) for _ in range(2)]
        self.o_qfm = [t(1024) for _ in range(8)]
        self.o_kfmA = [t(4096) for _ in range(2)]
        self.o_kfmB = [t(4096) for _ in range(2)]
        self.o_latbf = [t(512) for _ in range(2)]
        self.o_latfm = [t(512) for _ in range(2)]
        self.o_kbf = [t(512) for _ in range(2)]
        self.o_vsb = t(16 * 4 * 65 * 2)
        self.o_expT = [t(2048) for _ in range(2)]
        self.o_afm = [t(1024) for _ in range(8)]
        self.o_xt = [t(4096) for _ in range(2)]
        self.o_ut = [t(4096) for _ in range(2)]
        self.o_a1t = [t(4096) for _ in range(2)]
        self.o_scr = [t(4096) for _ in range(6)]
        self.o_att32 = [t(2048) for _ in range(2)]
        self.o_wA = t(16 * 1024)
        self.o_wB = t(16 * 1024)
        self.o_wC = t(8 * 1024)
        self.o_tab = [t(4096) for _ in range(4)]
        self.o_n1w = t(4096)
        self.o_n2w = t(4096)
        self.o_maskg = t(4096)
        self.o_maskl = t(2048)
        self.attn_peak = a.peak
        f = Alloc()
        tf = f.take
        self.f_h = [tf(2048) for _ in range(32)]
        self.f_g = [tf(2048) for _ in range(8)]
        self.f_g2 = [tf(2048) for _ in range(8)]
        self.f_gt = [tf(2048) for _ in range(2)]
        self.f_w1 = [tf(8192) for _ in range(2)]
        self.f_w2 = [tf(8192) for _ in range(2)]
        self.f_xtn = [tf(2048) for _ in range(8)]
        self.f_o2 = [tf(2048) for _ in range(2)]
        self.f_xti = [tf(2048) for _ in range(4)]
        self.f_xfm = [tf(2048) for _ in range(2)]
        self.f_im = tf(512)
        self.f_wsl = tf(512)
        self.f_scr = [tf(4096) for _ in range(4)]
        self.ffn_peak = f.peak
        assert max(a.peak, f.peak) <= 200704, (a.peak, f.peak)

    def at(self, off, ncols, dt):
        return self.AR.ap()[:, off:off + ncols * ISZ[dt]].bitcast(dt)

    def buf(self, key):
        if key not in self.bufs:
            self.bufs[key] = Buf(str(key))
        return self.bufs[key]

    def ps(self, off, n, dt=F32):
        if dt == F32:
            return self.pall.ap()[:, off:off + n]
        return self.pall.ap().bitcast(BF16)[:, 2 * off:2 * off + n]

    def psb(self, off, n, dt=F32):
        ncols = n if dt == F32 else (n + 1) // 2
        return tuple(self.buf(("bank", b))
                     for b in range(off // 512, (off + ncols + 511) // 512))

    def ld(self, dst, src, reads=(), writes=(), q="sync", waits=()):
        return self.p.op(q, lambda e: e.dma_start(out=dst, in_=src),
                         reads=reads, writes=writes, waits=waits, dma=True)

    st = ld

    def vop(self, fn, reads=(), writes=(), waits=()):
        return self.p.op("vector", fn, reads=reads, writes=writes, waits=waits)

    def aop(self, fn, reads=(), writes=(), waits=()):
        return self.p.op("scalar", fn, reads=reads, writes=writes, waits=waits)

    def top(self, fn, reads=(), writes=(), waits=()):
        return self.p.op("tensor", fn, reads=reads, writes=writes, waits=waits)

    def gop(self, fn, reads=(), writes=(), waits=()):
        return self.p.op("gpsimd", fn, reads=reads, writes=writes, waits=waits)

    def gdma(self, fn, reads=(), writes=(), waits=()):
        return self.p.op("gpsimd", fn, reads=reads, writes=writes, waits=waits,
                         dma=True)

    def mm_group(self, pieces, out_bufs, read_bufs):
        n = len(pieces)

        def fn(e):
            ins = None
            for i, (o, lh, r) in enumerate(pieces):
                ins = e.matmul(o, lh, r, start=(i == 0), stop=(i == n - 1))
            return ins
        return self.top(fn, reads=tuple(read_bufs), writes=tuple(out_bufs))

    def layer_norm(self, xin, uout, par, in_bufs, out_buf, scr_off, eps=1e-5):
        sm = self.small[par]
        scr = self.at(scr_off, 1024, F32)
        bscr = self.buf(("scr", scr_off))
        bs = self.buf(("small", par))
        self.vop(lambda e: e.tensor_reduce(sm["s1"][:], xin, axis=AX.X, op=ALU.add),
                 reads=tuple(in_bufs), writes=(bs,))
        self.vop(lambda e: e.tensor_scalar(sm["m"][:], sm["s1"][:], -1.0 / D, None,
                                           ALU.mult), reads=(bs,), writes=(bs,))
        self.vop(lambda e: e.tensor_scalar(scr, xin, sm["m"][:, 0:1], None, ALU.add),
                 reads=(bs,) + tuple(in_bufs), writes=(bscr,))
        self.aop(lambda e: e.activation(out=uout, in_=scr, func=AF.Square),
                 reads=(bscr,), writes=(out_buf,))
        self.vop(lambda e: e.tensor_reduce(sm["s2"][:], uout, axis=AX.X, op=ALU.add),
                 reads=(out_buf,), writes=(bs,))
        self.vop(lambda e: e.tensor_scalar(sm["s2"][:], sm["s2"][:], 1.0 / D, eps,
                                           ALU.mult, ALU.add), reads=(bs,), writes=(bs,))
        self.aop(lambda e: e.activation(out=sm["rstd1"][:], in_=sm["s2"][:],
                                        func=AF.Sqrt), reads=(bs,), writes=(bs,))
        self.vop(lambda e: e.reciprocal(sm["rstd1"][:], sm["rstd1"][:]),
                 reads=(bs,), writes=(bs,))
        return self.vop(lambda e: e.tensor_scalar(uout, scr, sm["rstd1"][:, 0:1],
                                                  None, ALU.mult),
                        reads=(bscr, bs), writes=(out_buf,))

    def rms_norm(self, xin, zout, par, wtile, wbuf, in_bufs, out_buf, scr_off):
        sm = self.small[par]
        scr = self.at(scr_off, 1024, F32)
        bscr = self.buf(("scr", scr_off))
        bs = self.buf(("small", par))
        self.aop(lambda e: e.activation(out=scr, in_=xin, func=AF.Square),
                 reads=tuple(in_bufs), writes=(bscr,))
        self.vop(lambda e: e.tensor_reduce(sm["s2"][:], scr, axis=AX.X, op=ALU.add),
                 reads=(bscr,), writes=(bs,))
        self.vop(lambda e: e.tensor_scalar(sm["s2"][:], sm["s2"][:], 1.0 / D, 1e-6,
                                           ALU.mult, ALU.add), reads=(bs,), writes=(bs,))
        self.aop(lambda e: e.activation(out=sm["rstd1"][:], in_=sm["s2"][:],
                                        func=AF.Sqrt), reads=(bs,), writes=(bs,))
        self.vop(lambda e: e.reciprocal(sm["rstd1"][:], sm["rstd1"][:]),
                 reads=(bs,), writes=(bs,))
        self.vop(lambda e: e.tensor_scalar(scr, xin, sm["rstd1"][:, 0:1], None,
                                           ALU.mult),
                 reads=(bs,) + tuple(in_bufs), writes=(bscr,))
        return self.vop(lambda e: e.tensor_tensor(zout, scr, wtile, ALU.mult),
                        reads=(bscr, wbuf), writes=(out_buf,))

    def transpose_blk(self, src, nchunk, ps_off, dst, src_bufs, dst_buf):
        pst = self.ps(ps_off, nchunk * 128, BF16)
        pbufs = self.psb(ps_off, nchunk * 128, BF16)

        def fn(e):
            ins = None
            for kc in range(nchunk):
                ins = e.transpose(pst[:, kc * 128:(kc + 1) * 128],
                                  src[:, kc * 128:(kc + 1) * 128],
                                  self.c["identb"][:])
            return ins
        self.top(fn, reads=tuple(src_bufs) + (self.buf(("c", "identb")),),
                 writes=pbufs)
        return self.vop(lambda e: e.tensor_copy(dst, pst), reads=pbufs,
                        writes=(dst_buf,))

    def qknorm_rstd(self, ps_in, ps_bufs, nheads, par, q_scaled, scr_off):
        sm = self.small[par]
        bs = self.buf(("small", par))
        sq = self.at(scr_off, 1024, F32)[:, 0:nheads * HD]
        bscr = self.buf(("scr", scr_off))
        self.aop(lambda e: e.activation(out=sq, in_=ps_in, func=AF.Square),
                 reads=tuple(ps_bufs), writes=(bscr,))
        self.vop(lambda e: e.tensor_reduce(
            sm["sumsq"][:, 0:nheads], sq.rearrange("p (h d) -> p h d", d=HD),
            axis=AX.X, op=ALU.add), reads=(bscr,), writes=(bs,))
        if q_scaled:
            self.vop(lambda e: e.tensor_scalar(sm["t16a"][:, 0:nheads],
                                               sm["sumsq"][:, 0:nheads], 1.0,
                                               HD * 1e-6, ALU.mult, ALU.add),
                     reads=(bs,), writes=(bs,))
        else:
            self.vop(lambda e: e.tensor_scalar(sm["t16a"][:, 0:nheads],
                                               sm["sumsq"][:, 0:nheads], 1.0 / HD,
                                               1e-6, ALU.mult, ALU.add),
                     reads=(bs,), writes=(bs,))
        self.aop(lambda e: e.activation(out=sm["t16b"][:, 0:nheads],
                                        in_=sm["t16a"][:, 0:nheads], func=AF.Sqrt),
                 reads=(bs,), writes=(bs,))
        self.vop(lambda e: e.reciprocal(sm["rstd"][:, 0:nheads],
                                        sm["t16b"][:, 0:nheads]),
                 reads=(bs,), writes=(bs,))

    def rope(self, ps_in, ps_bufs, nheads, par, tabc, tabs, tab_bufs, out_bf,
             out_buf, sa_off, sb_off):
        sm = self.small[par]
        bs = self.buf(("small", par))
        nc_ = nheads * HD
        pin3 = ps_in.rearrange("p (h d) -> p h d", d=HD)
        swap4 = ps_in.rearrange("p (h c two) -> p h c two", h=nheads,
                                two=2)[:, :, :, ::-1]
        cr3 = tabc.rearrange("p (o d) -> p o d", o=1).to_broadcast([128, nheads, HD])
        ss4 = tabs.rearrange("p (o c two) -> p o c two", o=1, two=2) \
            .to_broadcast([128, nheads, HD // 2, 2])
        sa = self.at(sa_off, 1024, F32)[:, 0:nc_]
        sb = self.at(sb_off, 1024, F32)[:, 0:nc_]
        ba = self.buf(("scr", sa_off))
        bb = self.buf(("scr", sb_off))
        sa3 = sa.rearrange("p (h d) -> p h d", d=HD)
        sb4 = sb.rearrange("p (h c two) -> p h c two", h=nheads, two=2)
        self.vop(lambda e: e.tensor_tensor(sa3, pin3, cr3, ALU.mult),
                 reads=tuple(ps_bufs) + tuple(tab_bufs), writes=(ba,))
        self.vop(lambda e: e.tensor_tensor(sb4, swap4, ss4, ALU.mult),
                 reads=tuple(ps_bufs) + tuple(tab_bufs), writes=(bb,))
        self.vop(lambda e: e.tensor_add(sa, sa, sb), reads=(ba, bb), writes=(ba,))
        rst3 = sm["rstd"][:, 0:nheads].rearrange("p (h o) -> p h o", o=1) \
            .to_broadcast([128, nheads, HD])
        return self.vop(lambda e: e.tensor_tensor(
            out_bf.rearrange("p (h d) -> p h d", d=HD), sa3, rst3, ALU.mult),
            reads=(ba, bs), writes=(out_buf,))

    def phase_consts(self):
        for n in ("identb", "ut", "iota16", "iota128", "ones64", "ones128"):
            src = getattr(self, "i_" + ("ut128" if n == "ut" else n))
            self.ld(self.c[n][:], src, writes=(self.buf(("c", n)),))
        self.gop(lambda e: e.reg_mov(self.r_ncap, NCAP - 1))
        self.gop(lambda e: e.reg_mov(self.r_ntok, NTOK - 1))
        self.gop(lambda e: e.memset(self.c["bigfill"][:], 1 << 20),
                 writes=(self.buf(("c", "bigfill")),))
        self.vop(lambda e: e.memset(self.c["ones_col"][:], 1.0),
                 writes=(self.buf(("c", "ones_col")),))
        self.vop(lambda e: e.memset(
            self.at(self.o_vsb, 16 * 4 * 65, BF16), 1.0),
            writes=(self.buf("vsb"),))
        for i, n in enumerate(("crg", "ssg", "crl", "ssl")):
            self.ld(self.at(self.o_tab[i], 1024, F32).rearrange(
                        "p (t d) -> p t d", d=HD),
                    getattr(self, "i_" + n).rearrange("(t p) d -> p t d", p=128),
                    writes=(self.buf(("tab", i)),))
        self.ld(self.at(self.o_n1w, 1024, F32), self.i_n1w, writes=(self.buf("n1w"),))
        self.ld(self.at(self.o_n2w, 1024, F32), self.i_n2w, writes=(self.buf("n2w"),))
        self.ld(self.at(self.o_maskg, 2048, BF16), self.i_maskg,
                writes=(self.buf("maskg"),))
        self.ld(self.at(self.o_maskl, 1024, BF16), self.i_maskl,
                writes=(self.buf("maskl"),))
        self.gdma(lambda e: e.dma_start(
            out=self.c["routb"].ap().rearrange("p (l kc e) -> p l kc e",
                                               l=NL, kc=8),
            in_=self.i_routers.rearrange("l (kc p) e -> p l kc e", p=128)),
            writes=(self.buf(("c", "routb")),))

    def phase_casts(self):
        for src, dstn in [("gwq", "wq"), ("gkvd", "kvd"), ("gwo", "wo"),
                          ("lwq", "lq"), ("lwk", "lk"), ("lwv", "lv"),
                          ("lwo", "lo"), ("gkup", "kup"), ("gvup", "vup")]:
            s = getattr(self, "i_" + src)
            d = self.wbh[dstn]
            self.gdma(lambda e, a=s, b=d: e.dma_start(out=b, in_=a),
                      writes=(self.buf(("wbh", dstn)),))
        w1s = self.i_w1p.rearrange("e g kc p c -> e (g kc p) c")
        w1d = self.w1b.rearrange("e g kc p c -> e (g kc p) c")
        w2s = self.i_w2p.rearrange("e dc k2 p c -> e (dc k2 p) c")
        w2d = self.w2b.rearrange("e dc k2 p c -> e (dc k2 p) c")
        for ei in range(NE):
            self.gdma(lambda e, a=w1s[ei], b=w1d[ei]: e.dma_start(out=b, in_=a),
                      writes=(self.buf(("w1b",)),))
            self.gdma(lambda e, a=w2s[ei], b=w2d[ei]: e.dma_start(out=b, in_=a),
                      writes=(self.buf(("w2b",)),))

    def wtile(self, off, name, hbmw):
        rows, cols = hbmw.shape
        v = self.at(off, rows // 128 * cols, BF16)
        b = self.buf(("wld", off))
        self.ld(v.rearrange("p (kc n) -> p kc n", n=cols),
                hbmw.rearrange("(kc p) n -> p kc n", p=128),
                reads=(self.buf(("wbh", name)),), writes=(b,))
        return v, b

    def kv_tail(self, t, par, k_ps, k_ps_bufs, v_ps, v_ps_bufs, tabi):
        vsb = self.at(self.o_vsb, 16 * 4 * 65, BF16)
        vdst = vsb[:, t * 260:(t + 1) * 260].rearrange(
            "p (h c) -> p h c", c=65)[:, :, 0:64]
        self.aop(lambda e: e.copy(vdst, v_ps.rearrange("p (h c) -> p h c", c=64)),
                 reads=tuple(v_ps_bufs), writes=(self.buf("vsb"),))
        self.qknorm_rstd(k_ps, k_ps_bufs, KVH, par, False, self.o_scr[4])
        kb = self.at(self.o_kbf[par], 256, BF16)
        b_kb = self.buf(("kbf", par))
        tabc = self.at(self.o_tab[tabi], 1024, F32)[:, t * 64:(t + 1) * 64]
        tabs = self.at(self.o_tab[tabi + 1], 1024, F32)[:, t * 64:(t + 1) * 64]
        self.rope(k_ps, k_ps_bufs, KVH, par, tabc, tabs,
                  (self.buf(("tab", tabi)), self.buf(("tab", tabi + 1))),
                  kb, b_kb, self.o_scr[par], self.o_scr[2 + par])
        pst = self.ps(3968, 256, BF16)
        pbufs = self.psb(3968, 256, BF16)

        def fnT(e):
            ins = None
            for kc in range(2):
                ins = e.transpose(pst[:, kc * 128:(kc + 1) * 128],
                                  kb[:, kc * 128:(kc + 1) * 128],
                                  self.c["identb"][:])
            return ins
        self.top(fnT, reads=(b_kb, self.buf(("c", "identb"))), writes=pbufs)
        kall = self.at(self.o_kfmA[0], 2 * 2048, BF16).rearrange(
            "p (c w) -> p c w", w=2048)
        bA0, bA1 = self.buf(("kfmA", 0)), self.buf(("kfmA", 1))
        self.vop(lambda e: e.tensor_copy(
            kall[:, :, t * 128:(t + 1) * 128],
            pst.rearrange("p (c w) -> p c w", w=128)),
            reads=pbufs, writes=(bA0, bA1))
        for ci in range(2):
            kA = self.at(self.o_kfmA[ci], 2048, BF16)
            bA = self.buf(("kfmA", ci))
            kB = self.at(self.o_kfmB[ci], 2048, BF16)
            bB = self.buf(("kfmB", ci))
            self.st(kB[0:64, t * 128:(t + 1) * 128],
                    kA[64:128, t * 128:(t + 1) * 128], reads=(bA,), writes=(bB,))
            self.st(kB[64:128, t * 128:(t + 1) * 128],
                    kA[0:64, t * 128:(t + 1) * 128], reads=(bA,), writes=(bB,))

    def attn_P1(self, s):
        tok0 = s * S
        wv_kvd, b_kvd = self.wtile(self.o_wC, "kvd", self.wbh["kvd"])
        wv_kup, b_kup = self.wtile(self.o_wC + 4096, "kup", self.wbh["kup"])
        wv_vup, b_vup = self.wtile(self.o_wC + 5120, "vup", self.wbh["vup"])
        for t in range(16):
            par = t % 2
            xt = self.at(self.o_xt[par], 1024, F32)
            ut = self.at(self.o_ut[par], 1024, F32)
            zt = self.at(self.o_qbf[par], 1024, BF16)
            bx = self.buf(("xt", par))
            bu = self.buf(("ut", par))
            bz = self.buf(("qbf", par))
            self.ld(xt, self.hid[tok0 + t * 128: tok0 + (t + 1) * 128, :],
                    writes=(bx,))
            self.layer_norm(xt, ut, par, (bx,), bu, self.o_scr[par])
            self.st(self.u_hbm[t * 128:(t + 1) * 128, :], ut, reads=(bu,),
                    writes=(self.buf("u_hbm"),))
            self.rms_norm(ut, zt, par, self.at(self.o_n1w, 1024, F32),
                          self.buf("n1w"), (bu,), bz, self.o_scr[2 + par])
            if self.dbg and s == 0:
                self.st(self.dbg_t["d_z1"][t * 128:(t + 1) * 128, :], zt,
                        reads=(bz,), q="gpsimd")
            zf = self.at(self.o_zfm[par], 1024, BF16)
            bzf = self.buf(("zfm", par))
            self.transpose_blk(zt, 8, 3072, zf, (bz,), bzf)
            self.st(self.z1fm_hbm[t * 128:(t + 1) * 128, :], zf, reads=(bzf,),
                    writes=(self.buf("z1fm_hbm"),))
            lat_ps = self.ps(2048 + par * 512, 256)
            lat_pb = self.psb(2048 + par * 512, 256)
            self.mm_group([(lat_ps, zf[:, kc * 128:(kc + 1) * 128],
                            wv_kvd[:, kc * 256:(kc + 1) * 256]) for kc in range(8)],
                          lat_pb, (bzf, b_kvd))
            latb = self.at(self.o_latbf[par], 256, BF16)
            b_latb = self.buf(("latbf", par))
            self.aop(lambda e, a=lat_ps, b=latb: e.copy(b, a), reads=lat_pb,
                     writes=(b_latb,))
            if self.dbg >= 3 and s == 0:
                self.st(self.dbg_t["d_lat"][t * 128:(t + 1) * 128, :], latb,
                        reads=(b_latb,), q="gpsimd")
                self.st(self.dbg_t["d_z1f"][t * 128:(t + 1) * 128, :], zf,
                        reads=(bzf,), q="gpsimd")
            latf = self.at(self.o_latfm[par], 256, BF16)
            b_latf = self.buf(("latfm", par))
            self.transpose_blk(latb, 2, 3840, latf, (b_latb,), b_latf)
            k_ps = self.ps(2048 + par * 512 + 256, 256)
            k_pb = self.psb(2048 + par * 512 + 256, 256)
            self.mm_group([(k_ps, latf[:, kc * 128:(kc + 1) * 128],
                            wv_kup[:, kc * 256:(kc + 1) * 256]) for kc in range(2)],
                          k_pb, (b_latf, b_kup))
            v_ps = self.ps(3584, 256)
            v_pb = self.psb(3584, 256)
            self.mm_group([(v_ps, latf[:, kc * 128:(kc + 1) * 128],
                            wv_vup[:, kc * 256:(kc + 1) * 256]) for kc in range(2)],
                          v_pb, (b_latf, b_vup))
            if self.dbg >= 3 and s == 0:
                kpre = self.at(self.o_scr[5], 1024, F32)[:, 0:256]
                b_kpre = self.buf(("scr", self.o_scr[5]))
                self.aop(lambda e, a=k_ps, b=kpre: e.copy(b, a), reads=k_pb,
                         writes=(b_kpre,))
                self.st(self.dbg_t["d_kpre"][t * 128:(t + 1) * 128, :], kpre,
                        reads=(b_kpre,), q="gpsimd")
            self.kv_tail(t, par, k_ps, k_pb, v_ps, v_pb, 0)
        if self.dbg and s == 0:
            for ci in range(2):
                self.st(self.dbg_t["d_kfm"][ci * 128:(ci + 1) * 128, :],
                        self.at(self.o_kfmA[ci], 2048, BF16),
                        reads=(self.buf(("kfmA", ci)),), q="gpsimd")
            self.st(self.dbg_t["d_vsb"], self.at(self.o_vsb, 16 * 4 * 65, BF16),
                    reads=(self.buf("vsb"),), q="gpsimd")

    def attn_qproj(self, t, t_in_blk, zsrc_hbm, zsrc_buf, wv_q, b_wq, tabi):
        par = t % 2
        zf = self.at(self.o_zfm[par], 1024, BF16)
        bzf = self.buf(("zfm", par))
        self.ld(zf, zsrc_hbm[t * 128:(t + 1) * 128, :], reads=(zsrc_buf,),
                writes=(bzf,))
        q_ps = self.ps(par * 1024, 1024)
        q_pb = self.psb(par * 1024, 1024)
        pieces = []
        for cb in range(2):
            for kc in range(8):
                pieces.append((q_ps[:, cb * 512:(cb + 1) * 512],
                               zf[:, kc * 128:(kc + 1) * 128],
                               wv_q[:, kc * 1024 + cb * 512: kc * 1024 + (cb + 1) * 512]))

        def fn(e):
            ins = None
            for i, (o, lh, r) in enumerate(pieces):
                ins = e.matmul(o, lh, r, start=(i % 8 == 0), stop=(i % 8 == 7))
            return ins
        self.top(fn, reads=(bzf, b_wq), writes=q_pb)
        if (self.dbg >= 3 and t == 0 and "d_qpre" in self.dbg_t
                and not getattr(self, "_qpre_dumped", False)):
            self._qpre_dumped = 1
            qpre = self.at(self.o_scr[5], 1024, F32)
            b_qpre = self.buf(("scr", self.o_scr[5]))
            self.aop(lambda e, a=q_ps, b=qpre: e.copy(b, a), reads=q_pb,
                     writes=(b_qpre,))
            self.st(self.dbg_t["d_qpre"][0:128, :], qpre, reads=(b_qpre,),
                    q="gpsimd")

        self.qknorm_rstd(q_ps, q_pb, H, par, True, self.o_scr[4 + par])
        qb = self.at(self.o_qbf[par], 1024, BF16)
        b_qb = self.buf(("qbf", par))
        tabc = self.at(self.o_tab[tabi], 1024, F32)[:, t * 64:(t + 1) * 64]
        tabs = self.at(self.o_tab[tabi + 1], 1024, F32)[:, t * 64:(t + 1) * 64]
        self.rope(q_ps, q_pb, H, par, tabc, tabs,
                  (self.buf(("tab", tabi)), self.buf(("tab", tabi + 1))),
                  qb, b_qb, self.o_scr[par], self.o_scr[2 + par])
        if (self.dbg >= 3 and "d_qrope" in self.dbg_t
                and getattr(self, "_qpre_dumped", False) == 1):
            self._qpre_dumped = 2
            self.st(self.dbg_t["d_qrope"][0:128, :], qb, reads=(b_qb,),
                    q="gpsimd")
        pst = self.ps(3072, 1024, BF16)
        pbufs = self.psb(3072, 1024, BF16)

        def fnT(e):
            ins = None
            for kc in range(8):
                ins = e.transpose(pst[:, kc * 128:(kc + 1) * 128],
                                  qb[:, kc * 128:(kc + 1) * 128],
                                  self.c["identb"][:])
            return ins
        self.top(fnT, reads=(b_qb, self.buf(("c", "identb"))), writes=pbufs)
        qall = self.at(self.o_qfm[0], 8 * 512, BF16).rearrange(
            "p (c w) -> p c w", w=512)
        self.vop(lambda e: e.tensor_copy(
            qall[:, :, t_in_blk * 128:(t_in_blk + 1) * 128],
            pst.rearrange("p (c w) -> p c w", w=128)),
            reads=pbufs,
            writes=tuple(self.buf(("qfm", c)) for c in range(8)))

    def sdpa_head(self, h, nch, first_kc, tile_w, local):
        kvh = h // 4
        bq = 64 * (h % 2)
        useA = (64 * (kvh % 2) == bq)
        kt = self.at((self.o_kfmA if useA else self.o_kfmB)[kvh // 2], 2048, BF16)
        b_kt = self.buf(("kfmA" if useA else "kfmB", kvh // 2))
        qfm = self.at(self.o_qfm[h // 2], 512, BF16)
        b_qfm = self.buf(("qfm", h // 2))
        vsb = self.at(self.o_vsb, 16 * 4 * 65, BF16)
        b_vsb = self.buf("vsb")
        hp = h % 2 if getattr(self, 'head_par', True) else 0
        pv_ps = self.ps(2048 + hp * 1024, 512)[0:65, 0:tile_w]
        pv_pb = self.psb(2048 + hp * 1024, 512)
        if local:
            off0 = 0 if nch == 4 else 512
            sc_par = (first_kc // 2 + 1) % 2 if nch == 4 else 0
            sc = self.ps(sc_par * 1024, 1024)
            sc_pb = self.psb(sc_par * 1024, 1024)

            def fn(e):
                ins = None
                for m in range(nch):
                    kc = first_kc + m
                    ins = e.matmul(sc[:, off0 + m * 256: off0 + (m + 1) * 256],
                                   kt[bq:bq + 64, kc * 128:(kc + 1) * 128],
                                   qfm[bq:bq + 64, 0:256], start=True, stop=True)
                return ins
            self.top(fn, reads=(b_kt, b_qfm), writes=sc_pb)
            ex = self.at(self.o_expT[sc_par], 1024, BF16)
            b_ex = self.buf(("expT", sc_par))
            self.aop(lambda e: e.activation(out=ex[:, off0:1024],
                                            in_=sc[:, off0:1024], func=AF.Exp),
                     reads=sc_pb, writes=(b_ex,))
            mk = self.at(self.o_maskl, 1024, BF16)
            self.vop(lambda e: e.tensor_tensor(ex[:, off0:1024], ex[:, off0:1024],
                                               mk[:, off0:1024], ALU.mult),
                     reads=(b_ex, self.buf("maskl")), writes=(b_ex,))

            def fn2(e):
                ins = None
                for i in range(nch):
                    kc = first_kc + i
                    ins = e.matmul(pv_ps,
                                   vsb[:, kc * 260 + kvh * 65: kc * 260 + (kvh + 1) * 65],
                                   ex[:, off0 + i * 256: off0 + (i + 1) * 256],
                                   start=(i == 0), stop=(i == nch - 1))
                return ins
            self.top(fn2, reads=(b_ex, b_vsb), writes=pv_pb)
        else:
            ngr = nch // 2
            for g in range(ngr):
                pg = g % 2
                sc = self.ps(pg * 1024, 1024)
                sc_pb = self.psb(pg * 1024, 1024)

                def fn(e, g=g, sc=sc):
                    ins = None
                    for i in range(2):
                        kc = first_kc + 2 * g + i
                        ins = e.matmul(sc[:, i * 512:(i + 1) * 512],
                                       kt[bq:bq + 64, kc * 128:(kc + 1) * 128],
                                       qfm[bq:bq + 64, 0:512], start=True, stop=True)
                    return ins
                self.top(fn, reads=(b_kt, b_qfm), writes=sc_pb)
                ex = self.at(self.o_expT[pg], 1024, BF16)
                b_ex = self.buf(("expT", pg))
                self.aop(lambda e, ex=ex, sc=sc: e.activation(
                    out=ex[:, 0:1024], in_=sc[:, 0:1024], func=AF.Exp),
                    reads=sc_pb, writes=(b_ex,))
                if g >= ngr - 2:
                    half = 0 if g == ngr - 2 else 1
                    mk = self.at(self.o_maskg, 2048, BF16)
                    self.vop(lambda e, hf=half, ex=ex, mk=mk: e.tensor_tensor(
                        ex[:, 0:1024], ex[:, 0:1024],
                        mk[:, hf * 1024:(hf + 1) * 1024], ALU.mult),
                        reads=(b_ex, self.buf("maskg")), writes=(b_ex,))

                def fn2(e, g=g, ex=ex):
                    ins = None
                    for i in range(2):
                        kc = first_kc + 2 * g + i
                        ins = e.matmul(pv_ps,
                                       vsb[:, kc * 260 + kvh * 65: kc * 260 + (kvh + 1) * 65],
                                       ex[:, i * 512:(i + 1) * 512],
                                       start=(g == 0 and i == 0),
                                       stop=(g == ngr - 1 and i == 1))
                    return ins
                self.top(fn2, reads=(b_ex, b_vsb), writes=pv_pb)
        rs_t = self.c["rsum"] if hp == 0 else self.c["rsum1"]
        b_rs = self.buf(("rsum", hp))
        self.vop(lambda e: e.reciprocal(rs_t[0:1, 0:tile_w],
                                        pv_ps[64:65, 0:tile_w]),
                 reads=pv_pb, writes=(b_rs,))
        bc_ps = self.ps(2560 + hp * 1024, 512)[0:64, 0:tile_w]
        bc_pb = self.psb(2560 + hp * 1024, 512)
        self.top(lambda e: e.matmul(bc_ps, self.c["ones64"][0:1, :],
                                    rs_t[0:1, 0:tile_w],
                                    start=True, stop=True),
                 reads=(b_rs, self.buf(("c", "ones64"))), writes=bc_pb)
        a32 = self.at(self.o_att32[h % 2], 512, F32)[0:64, 0:tile_w]
        b_a32 = self.buf(("att32", h % 2))
        self.aop(lambda e: e.copy(a32, pv_ps[0:64, 0:tile_w]), reads=pv_pb,
                 writes=(b_a32,))
        afm = self.at(self.o_afm[h // 2], 512, BF16)
        self.vop(lambda e: e.tensor_tensor(afm[bq:bq + 64, 0:tile_w], a32, bc_ps,
                                           ALU.mult),
                 reads=(b_a32,) + bc_pb, writes=(self.buf(("afm", h // 2)),))

    def attn_out_tile(self, t, t_in_blk, wv_o, b_wo, res_hbm, res_buf):
        par = t % 2
        pso = self.ps(par * 1024, 1024)
        pso_b = self.psb(par * 1024, 1024)
        pieces = []
        for cb in range(2):
            for kc in range(8):
                afm = self.at(self.o_afm[kc], 512, BF16)
                pieces.append((pso[:, cb * 512:(cb + 1) * 512],
                               afm[:, t_in_blk * 128:(t_in_blk + 1) * 128],
                               wv_o[:, kc * 1024 + cb * 512: kc * 1024 + (cb + 1) * 512]))

        def fn(e):
            ins = None
            for i, (o, lh, r) in enumerate(pieces):
                ins = e.matmul(o, lh, r, start=(i % 8 == 0), stop=(i % 8 == 7))
            return ins
        afm_bufs = tuple(self.buf(("afm", c)) for c in range(8))
        self.top(fn, reads=afm_bufs + (b_wo,), writes=pso_b)
        rt = self.at(self.o_ut[par], 1024, F32)
        b_rt = self.buf(("ut", par))
        self.ld(rt, res_hbm[t * 128:(t + 1) * 128, :], reads=(res_buf,),
                writes=(b_rt,))
        a1 = self.at(self.o_a1t[par], 1024, F32)
        b_a1 = self.buf(("a1t", par))
        self.vop(lambda e: e.tensor_tensor(a1, rt, pso, ALU.add),
                 reads=(b_rt,) + pso_b, writes=(b_a1,))
        return a1, b_a1

    def attn_P2(self, s):
        wv_q, b_wq = self.wtile(self.o_wA, "wq", self.wbh["wq"])
        wv_o, b_wo = self.wtile(self.o_wB, "wo", self.wbh["wo"])
        if getattr(self, "heavy_barrier", False):
            allw = [(self.qS, self.qS.n), (self.qA, self.qA.n),
                    (self.qG, self.qG.n), (self.tV, self.tV.n),
                    (self.tA, self.tA.n), (self.tP, self.tP.n),
                    (self.tG, self.tG.n)]
            for eng in ("tensor", "vector", "scalar", "gpsimd", "sync"):
                self.p.wait(eng, allw)
        for qj in range(4):
            for ti in range(4):
                self.attn_qproj(4 * qj + ti, ti, self.z1fm_hbm,
                                self.buf("z1fm_hbm"), wv_q, b_wq, 0)
            if self.dbg and s == 0 and qj == 0:
                for c in range(8):
                    self.st(self.dbg_t["d_qfm"][c * 128:(c + 1) * 128, :],
                            self.at(self.o_qfm[c], 512, BF16),
                            reads=(self.buf(("qfm", c)),), q="gpsimd")
            for h in range(H):
                self.sdpa_head(h, 4 * qj + 4, 0, 512, False)
            if self.dbg and s == 0:
                for c in range(8):
                    self.st(self.dbg_t["d_afm"][c * 128:(c + 1) * 128,
                                                qj * 512:(qj + 1) * 512],
                            self.at(self.o_afm[c], 512, BF16),
                            reads=(self.buf(("afm", c)),), q="gpsimd")
            for ti in range(4):
                t = 4 * qj + ti
                par = t % 2
                a1, b_a1 = self.attn_out_tile(t, ti, wv_o, b_wo, self.u_hbm,
                                              self.buf("u_hbm"))
                self.st(self.a1_hbm[t * 128:(t + 1) * 128, :], a1, reads=(b_a1,),
                        writes=(self.buf("a1_hbm"),))
                if self.dbg and s == 0:
                    self.st(self.dbg_t["d_a1"][t * 128:(t + 1) * 128, :], a1,
                            reads=(b_a1,), q="gpsimd")
                z2 = self.at(self.o_qbf[par], 1024, BF16)
                b_z2 = self.buf(("qbf", par))
                self.rms_norm(a1, z2, par, self.at(self.o_n2w, 1024, F32),
                              self.buf("n2w"), (b_a1,), b_z2, self.o_scr[2 + par])
                zf = self.at(self.o_zfm[par], 1024, BF16)
                bzf = self.buf(("zfm", par))
                self.transpose_blk(z2, 8, 3072, zf, (b_z2,), bzf)
                self.st(self.z2fm_hbm[t * 128:(t + 1) * 128, :], zf, reads=(bzf,),
                        writes=(self.buf("z2fm_hbm"),))

    def attn_P3a(self, s):
        wv_k, b_wk = self.wtile(self.o_wC, "lk", self.wbh["lk"])
        wv_v, b_wv = self.wtile(self.o_wC + 4096, "lv", self.wbh["lv"])
        for t in range(16):
            par = t % 2
            zf = self.at(self.o_zfm[par], 1024, BF16)
            bzf = self.buf(("zfm", par))
            self.ld(zf, self.z2fm_hbm[t * 128:(t + 1) * 128, :],
                    reads=(self.buf("z2fm_hbm"),), writes=(bzf,))
            k_ps = self.ps(2048 + par * 512 + 256, 256)
            k_pb = self.psb(2048 + par * 512 + 256, 256)
            self.mm_group([(k_ps, zf[:, kc * 128:(kc + 1) * 128],
                            wv_k[:, kc * 256:(kc + 1) * 256]) for kc in range(8)],
                          k_pb, (bzf, b_wk))
            v_ps = self.ps(3584, 256)
            v_pb = self.psb(3584, 256)
            self.mm_group([(v_ps, zf[:, kc * 128:(kc + 1) * 128],
                            wv_v[:, kc * 256:(kc + 1) * 256]) for kc in range(8)],
                          v_pb, (bzf, b_wv))
            self.kv_tail(t, par, k_ps, k_pb, v_ps, v_pb, 2)

    def attn_P3b(self, s):
        tok0 = s * S
        wv_q, b_wq = self.wtile(self.o_wA, "lq", self.wbh["lq"])
        wv_o, b_wo = self.wtile(self.o_wB, "lo", self.wbh["lo"])
        for M in range(8):
            for ti in range(2):
                self.attn_qproj(2 * M + ti, ti, self.z2fm_hbm,
                                self.buf("z2fm_hbm"), wv_q, b_wq, 2)
            nch = 2 if M == 0 else 4
            first_kc = 0 if M == 0 else 2 * M - 2
            for h in range(H):
                self.sdpa_head(h, nch, first_kc, 256, True)
            for ti in range(2):
                t = 2 * M + ti
                par = t % 2
                a2, b_a2 = self.attn_out_tile(t, ti, wv_o, b_wo, self.a1_hbm,
                                              self.buf("a1_hbm"))
                xt = self.at(self.o_xt[par], 1024, F32)
                bx = self.buf(("xt", par))
                self.ld(xt, self.hid[tok0 + t * 128: tok0 + (t + 1) * 128, :],
                        writes=(bx,))
                lnout = self.at(self.o_scr[4 + par], 1024, F32)
                b_ln = self.buf(("scr", self.o_scr[4 + par]))
                self.layer_norm(a2, lnout, par, (b_a2,), b_ln, self.o_scr[par])
                x1 = self.at(self.o_ut[par], 1024, F32)
                b_x1 = self.buf(("ut", par))
                self.vop(lambda e, a=x1, b=xt, c=lnout: e.tensor_tensor(
                    a, b, c, ALU.add), reads=(bx, b_ln), writes=(b_x1,))
                self.st(self.x1_hbm[tok0 + t * 128: tok0 + (t + 1) * 128, :], x1,
                        reads=(b_x1,), writes=(self.buf("x1_hbm"),))
                v0 = self.at(self.o_qbf[par], 1024, BF16)
                b_v0 = self.buf(("qbf", par))
                self.layer_norm(x1, v0, par, (b_x1,), b_v0, self.o_scr[par])
                self.st(self.xt_hbm[0][tok0 + t * 128: tok0 + (t + 1) * 128, :],
                        v0, reads=(b_v0,), writes=(self.buf(("xt_hbm", 0)),))
                if self.dbg:
                    self.st(self.dbg_t["d_x1"][tok0 + t * 128: tok0 + (t + 1) * 128, :],
                            x1, reads=(b_x1,), q="gpsimd")
                    self.st(self.dbg_t["d_xt0"][tok0 + t * 128: tok0 + (t + 1) * 128, :],
                            v0, reads=(b_v0,), q="gpsimd")

    def ffn_b1_tile(self, l, t, xin, b_xin, b_R, b_slots, b_wsb, b_gath,
                    b_im, b_ws):
        par = t % 2
        sm = self.small[par]
        bs = self.buf(("small", par))
        xr = 2 if getattr(self, "xti2", False) else 4
        xti = self.at(self.f_xti[t % xr], 1024, BF16)
        b_xti = self.buf(("f_xti", t % xr))
        self.ld(xti, xin[t * 128:(t + 1) * 128, :], reads=(b_xin,),
                writes=(b_xti,))
        xfm = self.at(self.f_xfm[par], 1024, BF16)
        b_xfm = self.buf(("f_xfm", par))
        self.transpose_blk(xti, 8, 3072, xfm, (b_xti,), b_xfm)
        lg_ps = self.ps(par * 512, NE)
        lg_pb = self.psb(par * 512, NE)
        self.mm_group([(lg_ps, xfm[:, kc * 128:(kc + 1) * 128],
                        self.c["routb"][:, (l * 8 + kc) * NE:(l * 8 + kc + 1) * NE])
                       for kc in range(8)],
                      lg_pb, (b_xfm, self.buf(("c", "routb"))))
        self.vop(lambda e: e.tensor_reduce(sm["m"][:], lg_ps, axis=AX.X,
                                           op=ALU.max, negate=True),
                 reads=lg_pb, writes=(bs,))
        self.aop(lambda e: e.activation(out=sm["pb"][:], in_=lg_ps, func=AF.Exp,
                                        bias=sm["m"][:, 0:1]),
                 reads=lg_pb + (bs,), writes=(bs,))
        self.vop(lambda e: e.tensor_reduce(sm["s1"][:], sm["pb"][:], axis=AX.X,
                                           op=ALU.add), reads=(bs,), writes=(bs,))
        self.vop(lambda e: e.reciprocal(sm["s2"][:], sm["s1"][:]),
                 reads=(bs,), writes=(bs,))
        self.vop(lambda e, t=t: e.tensor_copy(self.c["wsb"][:, t:t + 1],
                                              sm["s2"][:]),
                 reads=(bs,), writes=(b_wsb,))
        self.vop(lambda e: e.tensor_scalar(sm["mx"][:], sm["m"][:], -1.0, None,
                                           ALU.mult), reads=(bs,), writes=(bs,))
        self.vop(lambda e: e.tensor_scalar(sm["eq"][:], lg_ps, sm["mx"][:, 0:1],
                                           None, ALU.is_ge),
                 reads=lg_pb + (bs,), writes=(bs,))
        self.vop(lambda e: e.scalar_tensor_tensor(
            sm["t16a"][:], sm["eq"][:], -1000.0, self.c["iota16"][:],
            ALU.mult, ALU.add),
            reads=(bs, self.buf(("c", "iota16"))), writes=(bs,))
        self.vop(lambda e: e.tensor_reduce(sm["idx"][:], sm["t16a"][:],
                                           axis=AX.X, op=ALU.min),
                 reads=(bs,), writes=(bs,))
        self.vop(lambda e: e.tensor_scalar(sm["idx"][:], sm["idx"][:], 1.0,
                                           1000.0, ALU.mult, ALU.add),
                 reads=(bs,), writes=(bs,))
        self.vop(lambda e: e.tensor_scalar(sm["oh"][:], self.c["iota16"][:],
                                           sm["idx"][:, 0:1], None, ALU.is_equal),
                 reads=(bs, self.buf(("c", "iota16"))), writes=(bs,))
        pP = self.ps(1024 + par * 512, NE)
        pP_b = self.psb(1024 + par * 512, NE)

        def fnP(e):
            e.matmul(pP, self.c["ut"][:], sm["oh"][:], start=True, stop=False)
            return e.matmul(pP, self.c["ones128"][0:1, :], self.c["Rsb"][:],
                            start=False, stop=True)
        self.top(fnP, reads=(bs, b_R, self.buf(("c", "ut")),
                             self.buf(("c", "ones128"))), writes=pP_b)
        pC = self.ps(1024 + par * 512 + 256, NE)[0:1, :]
        pC_b = self.psb(1024 + par * 512 + 256, NE)
        self.top(lambda e: e.matmul(pC, self.c["ones_col"][:], sm["oh"][:],
                                    start=True, stop=True),
                 reads=(bs, self.buf(("c", "ones_col"))), writes=pC_b)
        self.vop(lambda e: e.tensor_tensor(sm["t16b"][:], sm["oh"][:], pP,
                                           ALU.mult),
                 reads=(bs,) + pP_b, writes=(bs,))
        self.vop(lambda e: e.tensor_reduce(sm["rank"][:], sm["t16b"][:],
                                           axis=AX.X, op=ALU.add),
                 reads=(bs,), writes=(bs,))
        self.vop(lambda e: e.tensor_tensor(self.c["Rsb"][:], self.c["Rsb"][:],
                                           pC, ALU.add),
                 reads=(b_R,) + pC_b, writes=(b_R,))
        self.vop(lambda e: e.tensor_scalar(sm["slot"][:], sm["idx"][:],
                                           float(CAP), sm["rank"][:, 0:1],
                                           ALU.mult, ALU.add),
                 reads=(bs,), writes=(bs,))
        self.vop(lambda e: e.tensor_scalar(sm["oflw"][:], sm["rank"][:],
                                           float(CAP), None, ALU.is_ge),
                 reads=(bs,), writes=(bs,))
        self.vop(lambda e: e.scalar_tensor_tensor(sm["slot"][:], sm["oflw"][:],
                                                  BIGSLOT, sm["slot"][:],
                                                  ALU.mult, ALU.add),
                 reads=(bs,), writes=(bs,))
        self.vop(lambda e, t=t: e.tensor_copy(self.c["slots"][:, t:t + 1],
                                              sm["slot"][:]),
                 reads=(bs,), writes=(b_slots,))
        self.vop(lambda e, t=t: e.tensor_scalar(sm["s1"][:],
                                                self.c["iota128"][:], 1.0,
                                                float(t * 128), ALU.mult,
                                                ALU.add),
                 reads=(self.buf(("c", "iota128")),), writes=(bs,))
        self.vop(lambda e: e.tensor_copy(sm["tok"][:], sm["s1"][:]),
                 reads=(bs,), writes=(bs,))
        if self.dbg >= 2 and l == getattr(self, 'dump_layer', 0):
            self.st(self.dbg_t["d_w"][t * 128:(t + 1) * 128, :], sm["s2"][:],
                    reads=(bs,), q="gpsimd")
            self.st(self.dbg_t["d_idx"][t * 128:(t + 1) * 128, :], sm["idx"][:],
                    reads=(bs,), q="gpsimd")
            self.st(self.dbg_t["d_rank"][t * 128:(t + 1) * 128, :],
                    sm["rank"][:], reads=(bs,), q="gpsimd")
        slot_ap = self.c["slots"][:, t:t + 1]
        b_slot_t = self.buf("slots" if getattr(self, "aggslots", True) else ("slots", t))
        self.gdma(lambda e, a=xti, sl=slot_ap: e.indirect_dma_start(
            out=self.gath, out_offset=IndirectOffsetOnAxis(ap=sl, axis=0),
            in_=a, in_offset=None, bounds_check=self.r_ncap, oob_is_err=False),
            reads=(b_xti, b_slot_t), writes=(b_gath,))
        self.gdma(lambda e, sm=sm, sl=slot_ap: e.indirect_dma_start(
            out=self.idxmap, out_offset=IndirectOffsetOnAxis(ap=sl, axis=0),
            in_=sm["tok"][:], in_offset=None,
            bounds_check=self.r_ncap, oob_is_err=False),
            reads=(bs, b_slot_t), writes=(b_im,))
        self.gdma(lambda e, t=t, sl=slot_ap: e.indirect_dma_start(
            out=self.wslot, out_offset=IndirectOffsetOnAxis(ap=sl, axis=0),
            in_=self.c["wsb"][:, t:t + 1], in_offset=None,
            bounds_check=self.r_ncap, oob_is_err=False),
            reads=(self.buf("wsb" if getattr(self, "aggslots", True) else ("wsb", t)), b_slot_t), writes=(b_ws,))

    def emit_gblock(self, l, e_i, b_gath):
        one_set = getattr(self, "one_gset", False)
        gslots = self.f_g if (e_i % 2 == 0 or one_set) else self.f_g2
        gkey = 0 if one_set else e_i % 2
        pso = 2048 if getattr(self, "gt_bank4", False) else 1024
        for i in range(8):
            gt = self.at(self.f_gt[i % 2], 1024, BF16)
            b_gt = self.buf(("f_gt", i % 2))
            self.ld(gt, self.gath[e_i * CAP + i * 128: e_i * CAP + (i + 1) * 128, :],
                    reads=(b_gath,), writes=(b_gt,))
            pst = self.ps(pso, 1024, BF16)
            pbufs = self.psb(pso, 1024, BF16)

            def fnT(e, gt=gt, pst=pst):
                ins = None
                for kc in range(8):
                    ins = e.transpose(pst[:, kc * 128:(kc + 1) * 128],
                                      gt[:, kc * 128:(kc + 1) * 128],
                                      self.c["identb"][:])
                return ins
            self.top(fnT, reads=(b_gt, self.buf(("c", "identb"))), writes=pbufs)
            gall = self.at(gslots[0], 8 * 1024, BF16).rearrange(
                "p (c w) -> p c w", w=1024)
            self.vop(lambda e, i=i, pst=pst, gall=gall: e.tensor_copy(
                gall[:, :, i * 128:(i + 1) * 128],
                pst.rearrange("p (c w) -> p c w", w=128)),
                reads=pbufs,
                writes=tuple(self.buf(("f_g", gkey, c)) for c in range(8)))

    def ffn_layer(self, l):
        xin = self.xt_hbm[l % 2]
        xout = self.xt_hbm[(l + 1) % 2]
        b_xin = self.buf(("xt_hbm", l % 2))
        b_xout = self.buf(("xt_hbm", (l + 1) % 2))
        b_gath = self.buf("gath")
        b_im = self.buf("idxmap")
        b_ws = self.buf("wslot")
        self.gdma(lambda e: e.dma_start(
            out=self.idxmap.rearrange("(p c) one -> p (c one)", p=128),
            in_=self.c["bigfill"][:]),
            reads=(self.buf(("c", "bigfill")),), writes=(b_im,))
        b_R = self.buf("Rsb")
        self.vop(lambda e: e.memset(self.c["Rsb"][:], 0.0), writes=(b_R,))
        b_slots = self.buf("slots")
        b_wsb = self.buf("wsb")
        for t in range(64):
            self.ffn_b1_tile(l, t, xin, b_xin, b_R, b_slots, b_wsb,
                             b_gath, b_im, b_ws)
        if not getattr(self, "seq_g", False):
            self.emit_gblock(l, 0, b_gath)
        for e_i in range(NE):
            if getattr(self, "seq_g", False):
                self.emit_gblock(l, e_i, b_gath)
            im8 = self.at(self.f_im, 8, I32)
            b_im8 = self.buf("f_im8")
            ws8 = self.at(self.f_wsl, 8, F32)
            b_ws8 = self.buf("f_ws8")
            for j in range(8):
                self.ld(im8[:, j:j + 1],
                        self.idxmap[e_i * CAP + j * 128: e_i * CAP + (j + 1) * 128, :],
                        reads=(b_im,), writes=(b_im8,))
                self.ld(ws8[:, j:j + 1],
                        self.wslot[e_i * CAP + j * 128: e_i * CAP + (j + 1) * 128, :],
                        reads=(b_ws,), writes=(b_ws8,))
            one_set = getattr(self, "one_gset", False)
            gslots = self.f_g if (e_i % 2 == 0 or one_set) else self.f_g2
            g_bufs = tuple(self.buf(("f_g", 0 if one_set else e_i % 2, kc))
                           for kc in range(8))
            for dffg in range(8):
                wv = self.at(self.f_w1[dffg % 2], 4096, BF16)
                b_w1 = self.buf(("f_w1", dffg % 2))
                self.ld(wv.rearrange("p (kc c) -> p kc c", c=512),
                        self.w1b[e_i, dffg].rearrange("kc p c -> p kc c"),
                        reads=(self.buf("w1b"),), writes=(b_w1,), q="scalar")
                for dci in range(4):
                    dffc = dffg * 4 + dci
                    par = dffc % 2
                    psh = self.ps(par * 1024, 1024)
                    psh_b = self.psb(par * 1024, 1024)
                    pieces = []
                    for half in range(2):
                        for kc in range(8):
                            gf = self.at(gslots[kc], 1024, BF16)
                            pieces.append((psh[:, half * 512:(half + 1) * 512],
                                           wv[:, kc * 512 + dci * 128:
                                              kc * 512 + dci * 128 + 128],
                                           gf[:, half * 512:(half + 1) * 512]))

                    def fn(e, pieces=pieces):
                        ins = None
                        for i, (o, lh, r) in enumerate(pieces):
                            ins = e.matmul(o, lh, r, start=(i % 8 == 0),
                                           stop=(i % 8 == 7))
                        return ins
                    self.top(fn, reads=(b_w1,) + g_bufs, writes=psh_b)
                    hb = self.at(self.f_h[dffc], 1024, BF16)
                    self.aop(lambda e, a=psh, b=hb: e.activation(
                        out=b, in_=a, func=AF.Silu),
                        reads=psh_b, writes=(self.buf(("f_h", dffc)),))
                    if self.dbg >= 3 and l == 0 and e_i == 0:
                        self.st(self.dbg_t["d_h"][dffc * 128:(dffc + 1) * 128, :],
                                hb, reads=(self.buf(("f_h", dffc)),), q="gpsimd")
            h_bufs = tuple(self.buf(("f_h", kc)) for kc in range(32))
            for dc in range(8):
                wv2 = self.at(self.f_w2[dc % 2], 4096, BF16)
                b_w2 = self.buf(("f_w2", dc % 2))
                self.ld(wv2.rearrange("p (k2 c) -> p k2 c", c=256),
                        self.w2b[e_i, dc].rearrange("k2 p c -> p k2 c"),
                        reads=(self.buf("w2b"),), writes=(b_w2,), q="scalar")
                par = dc % 2
                pso = self.ps(2048 + par * 1024, 1024)
                pso_b = self.psb(2048 + par * 1024, 1024)
                pieces = []
                for half in range(2):
                    for kc in range(32):
                        hb = self.at(self.f_h[kc], 1024, BF16)
                        pieces.append((pso[:, half * 512:(half + 1) * 512],
                                       wv2[:, (kc // 2) * 256 + (kc % 2) * 128:
                                           (kc // 2) * 256 + (kc % 2) * 128 + 128],
                                       hb[:, half * 512:(half + 1) * 512]))

                def fn2(e, pieces=pieces):
                    ins = None
                    for i, (o, lh, r) in enumerate(pieces):
                        ins = e.matmul(o, lh, r, start=(i % 32 == 0),
                                       stop=(i % 32 == 31))
                    return ins
                self.top(fn2, reads=(b_w2,) + h_bufs, writes=pso_b)
                o2 = self.at(self.f_o2[par], 1024, BF16)
                b_o2 = self.buf(("f_o2", par))
                self.aop(lambda e, a=pso, b=o2: e.copy(b, a), reads=pso_b,
                         writes=(b_o2,))
                if self.dbg >= 3 and l == 0 and e_i == 0 and dc == 0:
                    self.st(self.dbg_t["d_o2"], o2, reads=(b_o2,), q="gpsimd")
                pst = self.ps(par * 1024, 1024, BF16)
                pst_b = self.psb(par * 1024, 1024, BF16)

                def fnT(e, o2=o2, pst=pst):
                    ins = None
                    for j in range(8):
                        ins = e.transpose(pst[:, j * 128:(j + 1) * 128],
                                          o2[:, j * 128:(j + 1) * 128],
                                          self.c["identb"][:])
                    return ins
                self.top(fnT, reads=(b_o2, self.buf(("c", "identb"))), writes=pst_b)
                for j in range(8):
                    self.vop(lambda e, j=j, dc=dc, pst=pst: e.tensor_scalar(
                        self.at(self.f_xtn[j], 1024, BF16)[:, dc * 128:(dc + 1) * 128],
                        pst[:, j * 128:(j + 1) * 128], ws8[:, j:j + 1], None,
                        ALU.mult),
                        reads=pst_b + (b_ws8,), writes=(self.buf(("f_xtn", j)),))
                if (dc == 3 and e_i + 1 < NE
                        and not getattr(self, "seq_g", False)):
                    self.emit_gblock(l, e_i + 1, b_gath)
            if self.dbg >= 3 and l == 0 and e_i == 0:
                for j in range(8):
                    self.st(self.dbg_t["d_xtn"][j * 128:(j + 1) * 128, :],
                            self.at(self.f_xtn[j], 1024, BF16),
                            reads=(self.buf(("f_xtn", j)),), q="gpsimd")
            for j in range(8):
                xtn = self.at(self.f_xtn[j], 1024, BF16)
                self.gdma(lambda e, x=xtn, j=j: e.indirect_dma_start(
                    out=xout, out_offset=IndirectOffsetOnAxis(ap=im8[:, j:j + 1],
                                                              axis=0),
                    in_=x, in_offset=None,
                    bounds_check=self.r_ntok, oob_is_err=False),
                    reads=(self.buf(("f_xtn", j)), b_im8), writes=(b_xout,))
        if self.dbg >= 2:
            self.st(self.dbg_t[f"d_xt{l + 1}"], xout, reads=(b_xout,), q="gpsimd")

    def final(self):
        xt4 = self.xt_hbm[NL % 2]
        b_xt4 = self.buf(("xt_hbm", NL % 2))
        for t in range(64):
            par = t % 2
            f_bf = self.at(self.f_xti[par], 1024, BF16)
            b_f = self.buf(("f_xti", par))
            self.ld(f_bf, xt4[t * 128:(t + 1) * 128, :], reads=(b_xt4,),
                    writes=(b_f,))
            x1 = self.at(self.f_w1[par], 1024, F32)
            b_x1 = self.buf(("f_w1", par))
            self.ld(x1, self.x1_hbm[t * 128:(t + 1) * 128, :],
                    reads=(self.buf("x1_hbm"),), writes=(b_x1,))
            ff = self.at(self.f_w2[par], 1024, F32)
            b_ff = self.buf(("f_w2", par))
            self.vop(lambda e, a=f_bf, b=ff: e.tensor_copy(b, a), reads=(b_f,),
                     writes=(b_ff,))
            lnf = self.at(self.f_scr[2 + par], 1024, F32)
            b_lnf = self.buf(("scr", self.f_scr[2 + par]))
            self.layer_norm(ff, lnf, par, (b_ff,), b_lnf, self.f_scr[par])
            ot = self.at(self.f_xtn[2 * par], 1024, F32)
            b_ot = self.buf(("f_xtn2", par))
            self.vop(lambda e, a=x1, b=lnf, c=ot: e.tensor_tensor(c, a, b, ALU.add),
                     reads=(b_x1, b_lnf), writes=(b_ot,))
            self.st(self.out[t * 128:(t + 1) * 128, :], ot, reads=(b_ot,))

    def finish(self):
        p = self.p
        p.wait("sync", [(self.qS, self.qS.n), (self.qA, self.qA.n),
                        (self.qG, self.qG.n)])
        with self.nc.Block() as block:
            @block.sync
            def _(e):
                p.replay("sync", e)

            @block.scalar
            def _(e):
                p.replay("scalar", e)

            @block.vector
            def _(e):
                p.replay("vector", e)

            @block.tensor
            def _(e):
                p.replay("tensor", e)

            @block.gpsimd
            def _(e):
                p.replay("gpsimd", e)
        return self.nc


def build(dbg=0, stage=5):
    k = K(dbg)
    k.phase_consts()
    k.phase_casts()
    for s in range(NS):
        k.attn_P1(s)
        if stage >= 2:
            k.attn_P2(s)
        if stage >= 3:
            k.attn_P3a(s)
            k.attn_P3b(s)
        if stage < 2:
            break
    if stage >= 4:
        for l in range(NL):
            k.ffn_layer(l)
    if stage >= 5:
        k.final()
    else:
        for t in range(64):
            k.st(k.out[t * 128:(t + 1) * 128, :], k.at(k.o_xt[0], 1024, F32),
                 reads=(k.buf(("xt", 0)),))
    nc = k.finish()
    return nc, k


def make_in_maps(inputs):
    consts = host_consts(inputs)
    hid = np.ascontiguousarray(
        np.asarray(inputs["L0_attn_hidden"], np.float32).reshape(32 * S, D))
    base = {
        "gwq": inputs["L0_attn_g_wq"], "gkvd": inputs["L0_attn_g_kv_down"],
        "gkup": inputs["L0_attn_g_k_up"], "gvup": inputs["L0_attn_g_v_up"],
        "gwo": inputs["L0_attn_g_wo"], "lwq": inputs["L0_attn_l_wq"],
        "lwk": inputs["L0_attn_l_wk"], "lwv": inputs["L0_attn_l_wv"],
        "lwo": inputs["L0_attn_l_wo"], "routers": inputs["L0_ffn_routers"],
    }
    base = {k_: np.ascontiguousarray(np.asarray(v, np.float32))
            for k_, v in base.items()}
    base.update(consts)
    return [dict(base, hid=np.ascontiguousarray(hid[c * NTOK:(c + 1) * NTOK]))
            for c in range(NCORES)]


def kernel(**inputs):
    in_maps = make_in_maps(inputs)
    nc, _ = build(dbg=0, stage=5)
    res = run_bass_kernel_spmd(nc, in_maps, core_ids=list(range(NCORES)))
    return np.concatenate([r["out"] for r in res.results], axis=0).reshape(32, S, D)
